# version 05 (final)
# baseline (speedup 1.0000x reference)
import numpy as np
import ml_dtypes
import contextlib
import concourse.bass as bass
import concourse.bacc as bacc
import concourse.mybir as mybir
from concourse.bass_utils import run_bass_kernel_spmd

F32 = mybir.dt.float32
BF16 = mybir.dt.bfloat16
AF = mybir.ActivationFunctionType
ALU = mybir.AluOpType
AX = mybir.AxisListType
ET = mybir.EngineType

T = 8192
DM = 1024
DI = 2048
NST = 16
STT_ = 512
DH = 64
WIN = 256
FF = 2816
NEG = -1e9
STAGE = "full"


def host_consts():
    c = {}
    c["ident"] = np.eye(128, dtype=ml_dtypes.bfloat16)
    c["ones_row"] = np.ones((1, 512), dtype=np.float32)
    c["ones_row_bf"] = np.ones((1, 512), dtype=ml_dtypes.bfloat16)
    c["ones_128_bf"] = np.ones((1, 128), dtype=ml_dtypes.bfloat16)
    c["ones_128"] = np.ones((1, 128), dtype=np.float32)
    half = np.arange(0, DH, 2, dtype=np.float64)
    inv_freq = 1.0 / 10000.0 ** (half / DH)
    t = np.arange(T, dtype=np.float64)
    freqs = t[:, None] * inv_freq[None, :]
    freqs = np.concatenate([freqs, freqs], -1)
    base_scale = (half + 0.4 * DH) / (1.4 * DH)
    power = (t - T // 2) / (WIN // 2)
    scale = base_scale[None, :] ** power[:, None]
    scale = np.concatenate([scale, scale], -1)
    cos, sin = np.cos(freqs), np.sin(freqs)
    sgn = np.concatenate([-np.ones(DH // 2), np.ones(DH // 2)])[None, :]
    c["tabAq"] = (cos * scale * 8.0).astype(np.float32)
    c["tabBq"] = (sin * scale * sgn * 8.0).astype(np.float32)
    c["tabAk"] = (cos / scale).astype(np.float32)
    c["tabBk"] = (sin / scale * sgn).astype(np.float32)
    a = np.arange(128)[:, None]
    b = np.arange(384)[None, :]
    base = np.where((b >= a) & (b <= a + 256), 0.0, NEG)
    c["mask2"] = base.astype(ml_dtypes.bfloat16)
    c["mask1"] = np.where(b >= 128, base, NEG).astype(ml_dtypes.bfloat16)
    c["mask0"] = np.where(b >= 256, base, NEG).astype(ml_dtypes.bfloat16)
    p = np.arange(128)
    sel = np.zeros((16, 128, 128), dtype=np.float32)
    coll = np.zeros((16, 128, 128), dtype=np.float32)
    for j in range(16):
        sel[j, 8 * j + p // 16, p] = 1.0
        coll[j, p, 8 * j + p // 16] = 1.0
    c["selm"] = np.ascontiguousarray(sel.transpose(1, 0, 2).reshape(128, 16 * 128)).astype(ml_dtypes.bfloat16)
    c["collm"] = np.ascontiguousarray(coll.transpose(1, 0, 2).reshape(128, 16 * 128)).astype(ml_dtypes.bfloat16)
    ssel = np.zeros((16, 128), dtype=np.float32)
    ssel[p % 16, p] = 1.0
    c["ssel"] = ssel.astype(ml_dtypes.bfloat16)
    return c


def host_prep(inputs):
    g = {}
    g["nsc"] = inputs["L0_attn_norm_scale"].reshape(1, DM)
    g["ipw"] = inputs["L0_attn_in_proj_w"]
    g["ipb"] = inputs["L0_attn_in_proj_b"].reshape(1, 2 * DI)
    g["opw"] = inputs["L0_attn_out_proj_w"]
    g["opb"] = inputs["L0_attn_out_proj_b"].reshape(1, DM)
    g["dtiw"] = inputs["L0_attn_dt_in_w"]
    g["dtib"] = inputs["L0_attn_dt_in_b"].reshape(1, 64)
    g["dtow"] = inputs["L0_attn_dt_out_w"]
    g["dtob"] = inputs["L0_attn_dt_out_b"].reshape(1, DI)
    g["dtbias"] = inputs["L0_attn_dt_bias"].reshape(1, DI)
    g["bcw"] = np.concatenate([inputs["L0_attn_b_proj_w"], inputs["L0_attn_c_proj_w"]], axis=1)
    g["bcb"] = np.concatenate([inputs["L0_attn_b_proj_b"], inputs["L0_attn_c_proj_b"]]).reshape(1, 32)
    ck = inputs["L0_attn_conv_kernel"]
    ckc = np.zeros((128, 64), dtype=np.float32)
    for j in range(4):
        for cc in range(16):
            ckc[:, j * 16 + cc] = ck[j, cc * 128:(cc + 1) * 128]
    g["ckc"] = ckc
    g["cb"] = inputs["L0_attn_conv_bias"].reshape(1, DI)
    al = inputs["L0_attn_a_log"]
    ac = np.zeros((128, 256), dtype=np.float32)
    pp = np.arange(128)
    for gi in range(256):
        dg, j = gi // 16, gi % 16
        ac[:, gi] = al[dg * 128 + 8 * j + pp // 16, pp % 16]
    g["a_log_cols"] = ac
    g["d_cols"] = np.ascontiguousarray(inputs["L0_attn_d"].reshape(16, 128).T)
    for l in range(6):
        g[f"wqkv{l}"] = inputs["L0_ffn_Wqkv"][l]
        g[f"wo{l}"] = inputs["L0_ffn_Wo"][l]
        w1 = inputs["L0_ffn_W1"][l]
        pad = np.zeros((DM, FF - 2730), dtype=np.float32)
        g[f"w1p{l}"] = np.concatenate([w1[:, :2730], pad, w1[:, 2730:], pad], axis=1)
        b1 = inputs["L0_ffn_b1"][l]
        bpad = np.zeros(FF - 2730, dtype=np.float32)
        g[f"b1p{l}"] = np.concatenate([b1[:2730], bpad, b1[2730:], bpad]).reshape(1, 2 * FF)
        w2 = inputs["L0_ffn_W2"][l]
        g[f"w2p{l}"] = np.concatenate([w2, np.zeros((FF - 2730, DM), dtype=np.float32)], axis=0)
        g[f"b2{l}"] = inputs["L0_ffn_b2"][l].reshape(1, DM)
        g[f"lng{l}"] = inputs["L0_ffn_ln_g"][l].reshape(1, DM)
        g[f"lnb{l}"] = inputs["L0_ffn_ln_b"][l].reshape(1, DM)
        g[f"qs{l}"] = inputs["L0_ffn_q_scale"][l].reshape(1, DH)
        g[f"ks{l}"] = inputs["L0_ffn_k_scale"][l].reshape(1, DH)
    return g


def dap(t, off, dims):
    return bass.AP(t.tensor if isinstance(t, bass.AP) else t, off, dims)


def build(stage):
    do_compile = [False]
    nc = bacc.Bacc("TRN2", target_bir_lowering=False, debug=False)
    EI, EO = "ExternalInput", "ExternalOutput"
    D = nc.dram_tensor
    x_in = D("x_in", [T, DM], F32, kind=EI).ap()
    H = {}
    for name, shp in [
        ("nsc", [1, DM]), ("ipw", [DM, 2 * DI]), ("ipb", [1, 2 * DI]),
        ("opw", [DI, DM]), ("opb", [1, DM]), ("dtiw", [DI, 64]), ("dtib", [1, 64]),
        ("dtow", [64, DI]), ("dtob", [1, DI]), ("dtbias", [1, DI]),
        ("bcw", [DI, 32]), ("bcb", [1, 32]), ("ckc", [128, 64]), ("cb", [1, DI]),
        ("a_log_cols", [128, 256]), ("d_cols", [128, 16]),
        ("ones_row", [1, 512]), ("ones_128", [1, 128]),
        ("tabAq", [T, DH]), ("tabBq", [T, DH]), ("tabAk", [T, DH]), ("tabBk", [T, DH]),
    ]:
        H[name] = D(name, shp, F32, kind=EI).ap()
    for name, shp in [
        ("ident", [128, 128]), ("mask0", [128, 384]), ("mask1", [128, 384]),
        ("mask2", [128, 384]), ("selm", [128, 16 * 128]), ("collm", [128, 16 * 128]),
        ("ssel", [16, 128]), ("ones_row_bf", [1, 512]), ("ones_128_bf", [1, 128]),
    ]:
        H[name] = D(name, shp, BF16, kind=EI).ap()
    for l in range(6):
        for name, shp in [
            (f"wqkv{l}", [DM, 3 * DM]), (f"wo{l}", [DM, DM]), (f"w1p{l}", [DM, 2 * FF]),
            (f"b1p{l}", [1, 2 * FF]), (f"w2p{l}", [FF, DM]), (f"b2{l}", [1, DM]),
            (f"lng{l}", [1, DM]), (f"lnb{l}", [1, DM]), (f"qs{l}", [1, DH]), (f"ks{l}", [1, DH]),
        ]:
            H[name] = D(name, shp, F32, kind=EI).ap()
    out = D("out", [T, DM], F32, kind=EO).ap()
    xn_s = D("xn_s", [T, DM], F32).ap()
    uT_s = D("uT_s", [DI, T], BF16).ap()
    zsT_s = D("zsT_s", [DI, T], BF16).ap()
    dtT_s = D("dtT_s", [DI, T], BF16).ap()
    bT_s = D("bT_s", [16, T], BF16).ap()
    cT_s = D("cT_s", [16, T], BF16).ap()
    ygT_s = D("ygT_s", [DI, T], BF16).ap()
    xres = D("xres", [T, DM], F32).ap()
    xinner = D("xinner", [T, DM], F32).ap()

    ctx = contextlib.ExitStack()
    sb = lambda n, s, d2=F32: ctx.enter_context(nc.sbuf_tensor(n, s, d2))
    ps = lambda n, s, d2=F32: ctx.enter_context(nc.psum_tensor(n, s, d2))
    sem = lambda n: ctx.enter_context(nc.semaphore(n))

    with ctx:
        c_ident = sb("c_ident", [128, 128], BF16)
        c_ones512 = sb("c_ones512", [1, 512], BF16)
        c_ones128 = sb("c_ones128", [1, 128], BF16)
        c_onesf = sb("c_onesf", [1, 128], F32)
        c_masks = sb("c_masks", [128, 3 * 384], BF16)
        c_ssel = sb("c_ssel", [16, 128], BF16)
        c_acols = sb("c_acols", [128, 256], F32)
        c_dcols = sb("c_dcols", [128, 16], F32)
        kcols = sb("kcols", [128, 64], F32)
        f_stage = sb("f_stage", [1, 256], F32)
        bb_a = sb("bb_a", [1, 4192], BF16)
        r_qsA = sb("r_qsA", [128, DH], F32)
        r_qsB = sb("r_qsB", [128, DH], F32)
        r_ksA = sb("r_ksA", [128, DH], F32)
        r_ksB = sb("r_ksB", [128, DH], F32)
        st1 = sb("st1", [128, 16], F32)
        st2 = sb("st2", [128, 4], F32)
        st3 = sb("st3", [128, 4], F32)
        st4 = sb("st4", [128, 4], F32)
        hstate = sb("hstate", [128, 256], F32)
        smax = sb("smax", [128, 4], F32)
        szv = sb("szv", [128, 4], F32)
        srp = sb("srp", [128, 4], F32)
        tailb = sb("tailb", [128, 48], BF16)
        ec5 = sb("ec5", [128, 1], F32)
        ec6 = sb("ec6", [128, 1], F32)
        ec24 = sb("ec24", [128, 1], F32)
        onec = sb("onec", [128, 1], F32)
        mhc = sb("mhc", [128, 1], F32)
        wA = sb("wA", [128, 24576], BF16)
        wB = sb("wB", [128, 22528], BF16)
        wC = sb("wC", [128, 27648], BF16)
        wF = sb("wF", [128, 10240], F32)
        msbT_s = D("msbT_s", [FF, T], BF16).ap()
        xt = wF[:, 0:4096]
        xt2 = wF[:, 4096:8192]
        dump = wF[:, 8192:9216]
        c_nsrep = wF[:, 8192:9216]
        spw = wF[:, 0:1536]
        r_lng = wF[:, 8192:9216]
        r_lnb = wF[:, 9216:10240]
        tabt = wF[:, 9216:9344]
        rot2 = wF[:, 9344:9472]
        sc_da = wF[:, 0:1024]
        sc_db = wF[:, 1024:2048]
        sc_h = wF[:, 2048:3072]
        sc_y = wF[:, 3072:3584]
        xb = wC[:, 0:4096]
        qT = wC[:, 0:4096]
        xsT = wC[:, 4096:8192]
        woT = wC[:, 4096:8192]
        kwin = wC[:, 8192:14336]
        vbuf = wC[:, 14336:20480]
        uprev = wC[:, 8192:16432]
        qkt_b = wC[:, 20480:22528]
        qp_t = wC[:, 22528:24576]
        zbuf = wC[:, 20480:22528]
        A4dmp = wC[:, 20480:24576]
        asb2 = wC[:, 20480:21504]
        gsb2 = wC[:, 21504:22528]
        mso2 = wC[:, 22528:24576]
        xpb = wC[:, 24576:26624]
        qn_b = wC[:, 24576:25600]
        qd_b = wC[:, 25600:26624]
        qr_b = wC[:, 26624:27648]
        pn = wC[:, 24576:25344]
        attnT = wC[:, 25344:26880]
        ygt = wC[:, 0:8192]
        sc_dt = wC[:, 0:512]
        sc_u = wC[:, 512:1024]
        sc_zs = wC[:, 1024:1536]
        sc_u2 = wC[:, 1536:2048]
        sc_u3 = wC[:, 2048:3072]
        sc_q = wC[:, 3072:4096]
        sc_g = wC[:, 4096:4608]
        db2row = wC[0:1, 16432:18480]
        cbrow = wC[0:1, 24576:26624]
        dtpre = wC[0:64, 18480:18992]
        sc_bt = wC[0:16, 4608:5120]
        sc_ct = wC[0:16, 5120:5632]
        b1row = wC[0:1, 8192:13824]
        b2row = wC[0:1, 13824:14848]
        oprow = wB[0:1, 19968:20992]

        def ust(m):
            return uprev[:, m * 515 + 3: m * 515 + 515]

        def vb(i):
            return vbuf[:, i * 1024:(i + 1) * 1024]

        pA = ps("pA", [128, 512], F32)
        pB = ps("pB", [128, 512], F32)
        pC = ps("pC", [128, 512], F32)
        pD = ps("pD", [128, 512], F32)
        pE = ps("pE", [128, 512], F32)
        pF = ps("pF", [128, 512], F32)
        pT1 = ps("pT1", [128, 512], BF16)
        pT2 = ps("pT2", [128, 512], BF16)
        PSF = [pA, pB, pC, pD, pE, pF]

        sD = sem("sD")
        sG = sem("sG")
        sT = sem("sT")
        sV = sem("sV")
        sS = sem("sS")
        sE = sem("sE")
        SEMS = [sD, sG, sT, sV, sS, sE]
        cnt = {}
        syq, gpq, te, ve, se = nc.sync, nc.gpsimd, nc.tensor, nc.vector, nc.scalar
        ALLE = [ET.SP, ET.PE, ET.DVE, ET.Activation, ET.Pool]

        def clr():
            nc.all_engine_barrier()
            for s_ in SEMS:
                nc.sync.sem_clear(s_)
                cnt[s_] = 0
            nc.all_engine_barrier()

        def _wrap_eng(eng, sem_, methods):
            for mname in methods:
                f = getattr(eng, mname)

                def mk(f2):
                    def w(*a, **k):
                        eng.wait_ge(sem_, cnt[sem_])
                        r = f2(*a, **k)
                        r.then_inc(sem_, 1)
                        cnt[sem_] += 1
                        return r
                    return w
                setattr(eng, mname, mk(f))

        _wrap_eng(ve, sV, ["tensor_copy", "tensor_tensor", "tensor_add", "tensor_sub",
                           "tensor_mul", "tensor_max", "tensor_scalar", "tensor_scalar_mul",
                           "tensor_scalar_max", "tensor_scalar_min", "tensor_scalar_add",
                           "tensor_single_scalar", "reduce_sum", "reduce_max", "memset",
                           "reciprocal", "scalar_tensor_tensor", "tensor_tensor_scan"])
        _wrap_eng(se, sS, ["activation", "copy", "mul", "add"])

        def inc(instr, s_, n=1):
            if s_ is sV or s_ is sS:
                return cnt[s_]
            instr.then_inc(s_, n)
            cnt[s_] += n
            return cnt[s_]

        def v(s_):
            return cnt[s_]

        _last_te = [None]
        _orig_mm = te.matmul
        _orig_tp = te.transpose

        def _mmw(*a, **k):
            r = _orig_mm(*a, **k)
            _last_te[0] = r
            return r

        def _tpw(*a, **k):
            r = _orig_tp(*a, **k)
            _last_te[0] = r
            return r
        te.matmul = _mmw
        te.transpose = _tpw

        def tmark():
            _last_te[0].then_inc(sT, 1)
            cnt[sT] += 1
            return cnt[sT]

        def bias_part(bank_ap, row_bf, n=512):
            te.matmul(bank_ap, row_bf, c_ones512[0:1, 0:n], start=True, stop=False)

        def bias_free(bank_ap, row_bf):
            te.matmul(bank_ap, c_ones128[:], row_bf, start=True, stop=False)

        def repl128(dst, dram_row, width):
            done = 0
            while done < width:
                w = min(256, width - done)
                dv = inc(syq.dma_start(out=f_stage[0:1, 0:w], in_=dram_row[0:1, done:done + w]), sD, 16)
                te.wait_ge(sD, dv)
                te.matmul(pA[:, 0:w], c_onesf[:], f_stage[0:1, 0:w], start=True, stop=True)
                tv = tmark()
                ve.wait_ge(sT, tv)
                vv = inc(ve.tensor_copy(dst[:, done:done + w], pA[:, 0:w]), sV)
                te.wait_ge(sV, vv)
                syq.wait_ge(sV, vv)
                done += w

        def wload(dst, src_h, C, n):
            src3 = src_h.rearrange("(c p) n -> p c n", p=128)
            for c in range(C):
                dv = inc(syq.dma_start(out=wF[:, 0:n], in_=src3[:, c, :]), sD, 16)
                ve.wait_ge(sD, dv)
                vv = inc(ve.tensor_copy(dst[:, c * n:(c + 1) * n], wF[:, 0:n]), sV)
                syq.wait_ge(sV, vv)

        def rowload(dst_row, src_h, width):
            done = 0
            while done < width:
                w = min(4096, width - done)
                dv = inc(syq.dma_start(out=wF[0:1, 0:w], in_=src_h[0:1, done:done + w]), sD, 16)
                ve.wait_ge(sD, dv)
                vv = inc(ve.tensor_copy(dst_row[0:1, done:done + w], wF[0:1, 0:w]), sV)
                syq.wait_ge(sV, vv)
                done += w

        def ln_stats(src_ap, q, eps, mu_t, rs_t, dmp):
            se.wait_ge(sV, v(sV))
            inc(ve.reduce_sum(mu_t[:, q:q + 1], src_ap, axis=AX.X), sV)
            inc(se.activation(dmp, src_ap, AF.Square, accum_out=rs_t[:, q:q + 1]), sS)
            ve.wait_ge(sS, v(sS))
            inc(ve.tensor_scalar_mul(mu_t[:, q:q + 1], mu_t[:, q:q + 1], 1.0 / DM), sV)
            inc(ve.tensor_scalar_mul(rs_t[:, q:q + 1], rs_t[:, q:q + 1], 1.0 / DM), sV)
            inc(ve.tensor_tensor(st4[:, 0:1], mu_t[:, q:q + 1], mu_t[:, q:q + 1], ALU.mult), sV)
            inc(ve.tensor_sub(rs_t[:, q:q + 1], rs_t[:, q:q + 1], st4[:, 0:1]), sV)
            se.wait_ge(sV, v(sV))
            inc(se.activation(rs_t[:, q:q + 1], rs_t[:, q:q + 1], AF.Ln, bias=eps), sS)
            inc(se.activation(rs_t[:, q:q + 1], rs_t[:, q:q + 1], AF.Exp, scale=mhc[:]), sS)
            ve.wait_ge(sS, v(sS))

        P0ON = stage != "p2n"
        clr()
        if P0ON:
            inc(ve.memset(ec5[:], 1e-5), sV)
            inc(ve.memset(ec6[:], 1e-6), sV)
            inc(ve.memset(ec24[:], 1e-24), sV)
            inc(ve.memset(onec[:], 1.0), sV)
            inc(ve.memset(mhc[:], -0.5), sV)
            inc(syq.dma_start(out=c_acols[:], in_=H["a_log_cols"]), sD, 16)
            inc(syq.dma_start(out=c_dcols[:], in_=H["d_cols"]), sD, 16)
            inc(syq.dma_start(out=kcols[:], in_=H["ckc"]), sD, 16)
            inc(syq.dma_start(out=c_onesf[:], in_=H["ones_128"]), sD, 16)
            inc(syq.dma_start(out=c_ident[:], in_=H["ident"]), sD, 16)
            inc(syq.dma_start(out=c_ones512[:], in_=H["ones_row_bf"]), sD, 16)
            inc(syq.dma_start(out=c_ones128[:], in_=H["ones_128_bf"]), sD, 16)
            for i, mk in enumerate(["mask0", "mask1", "mask2"]):
                inc(syq.dma_start(out=c_masks[:, i * 384:(i + 1) * 384], in_=H[mk]), sD, 16)
            inc(syq.dma_start(out=c_ssel[:], in_=H["ssel"]), sD, 16)
            inc(gpq.dma_start(out=wA[:, 0:16384].rearrange("p (c n) -> p c n", n=2048),
                              in_=H["ipw"].rearrange("(c p) n -> p c n", p=128)[:, :, 0:2048]), sG, 16)
            inc(gpq.dma_start(out=wB[:, 0:16384].rearrange("p (c n) -> p c n", n=2048),
                              in_=H["ipw"].rearrange("(c p) n -> p c n", p=128)[:, :, 2048:4096]), sG, 16)
            inc(gpq.dma_start(out=wB[:, 16384:16896].rearrange("p (c n) -> p c n", n=32), in_=H["bcw"].rearrange("(c p) n -> p c n", p=128)), sG, 16)
            inc(gpq.dma_start(out=wB[0:64, 16896:18944], in_=H["dtow"]), sG, 16)
            inc(gpq.dma_start(out=wB[:, 18944:19968].rearrange("p (c n) -> p c n", n=64), in_=H["dtiw"].rearrange("(c p) n -> p c n", p=128)), sG, 16)
            inc(gpq.dma_start(out=oprow[:, 0:1024], in_=H["opb"]), sG, 16)
            inc(gpq.dma_start(out=bb_a[0:1, 0:4096], in_=H["ipb"]), sG, 16)
            inc(gpq.dma_start(out=bb_a[0:1, 4096:4160], in_=H["dtib"]), sG, 16)
            inc(gpq.dma_start(out=bb_a[0:1, 4160:4192], in_=H["bcb"]), sG, 16)
            inc(gpq.dma_start(out=db2row[:, 0:2048], in_=H["dtob"]), sG, 16)
            inc(gpq.dma_start(out=wC[0:1, 0:2048], in_=H["dtbias"]), sG, 16)
            inc(gpq.dma_start(out=cbrow[:, 0:2048], in_=H["cb"]), sG, 16)
            ve.wait_ge(sD, v(sD))
            ve.wait_ge(sG, v(sG))
            inc(ve.tensor_add(db2row[:, 0:2048], db2row[:, 0:2048], wC[0:1, 0:2048]), sV)
            se.wait_ge(sD, v(sD))
            inc(se.activation(c_acols[:], c_acols[:], AF.Exp), sS)
            ve.wait_ge(sS, v(sS))
            inc(ve.tensor_scalar_mul(c_acols[:], c_acols[:], -1.0), sV)
            for idx in range(64):
                inc(ve.tensor_scalar_mul(wA[:, 16384 + idx * 128:16384 + (idx + 1) * 128],
                                         c_ident[:], kcols[:, idx:idx + 1]), sV)
            inc(ve.memset(hstate[:], 0.0), sV)
            inc(ve.memset(tailb[:], 0.0), sV)
            te.wait_ge(sV, v(sV))
        nc.all_engine_barrier()
        if P0ON:
            repl128(c_nsrep, H["nsc"], 1024)
        clr()

        if stage.startswith("p1") or stage in ("p2", "mamba", "full"):
          with nc.Fori(0, NST) as st:
            clr()
            xoff = st * (STT_ * DM)
            for q in range(4):
                inc(syq.dma_start(out=xt[:, q * DM:(q + 1) * DM],
                                  in_=dap(x_in, xoff + q * 128 * DM, [[DM, 128], [1, DM]])), sD, 16)
            ve.wait_ge(sD, v(sD))
            se.wait_ge(sD, v(sD))
            for q in range(4):
                ln_stats(xt[:, q * DM:(q + 1) * DM], q, ec5[:], st1, st2, A4dmp[:, 0:1024])
                inc(ve.tensor_scalar(xt2[:, q * DM:(q + 1) * DM], xt[:, q * DM:(q + 1) * DM],
                                     st1[:, q:q + 1], st2[:, q:q + 1], ALU.subtract, ALU.mult), sV)
            syq.wait_ge(sV, v(sV))
            for q in range(4):
                inc(syq.dma_start(out=dap(xn_s, xoff + q * 128 * DM, [[DM, 128], [1, DM]]),
                                  in_=xt2[:, q * DM:(q + 1) * DM]), sE, 16)
            se.wait_ge(sV, v(sV))
            for q in range(4):
                xnq = xt2[:, q * DM:(q + 1) * DM]
                inc(se.activation(A4dmp[:, 0:1024], xnq, AF.Square, accum_out=st3[:, q:q + 1]), sS)
            ve.wait_ge(sS, v(sS))
            inc(ve.tensor_scalar_mul(st3[:, 0:4], st3[:, 0:4], 1.0 / DM), sV)
            se.wait_ge(sV, v(sV))
            inc(se.activation(st3[:, 0:4], st3[:, 0:4], AF.Ln, bias=ec6[:]), sS)
            inc(se.activation(st3[:, 0:4], st3[:, 0:4], AF.Exp, scale=mhc[:]), sS)
            ve.wait_ge(sS, v(sS))
            for q in range(4):
                inc(ve.tensor_scalar_mul(xt[:, q * DM:(q + 1) * DM], xt2[:, q * DM:(q + 1) * DM],
                                         st3[:, q:q + 1]), sV)
                inc(ve.tensor_mul(xb[:, q * DM:(q + 1) * DM], xt[:, q * DM:(q + 1) * DM], c_nsrep[:]), sV)
            nc.multi_engine_barrier(ALLE)
            P1GO = {"p1a": 0, "p1t": 0.5, "p1u": 0.75, "p1z": 1, "p1b": 1, "p1c": 2, "p1d": 3}.get(stage, 9)
            P1GO = P1GO if stage.startswith("p1") and stage != "p1" else 9
            for cc in range(8 if P1GO >= 0.5 else 0):
                for q in range(4):
                    te.transpose(pT1[:, q * 128:(q + 1) * 128],
                                 xb[:, q * DM + cc * 128: q * DM + (cc + 1) * 128], c_ident[:])
                tv = tmark()
                ve.wait_ge(sT, tv)
                inc(ve.tensor_copy(xsT[:, cc * 512:(cc + 1) * 512], pT1[:]), sV)
                te.wait_ge(sV, v(sV))
            nc.multi_engine_barrier(ALLE)
            evdone = {}
            zdma = {}
            for m in range((16 if P1GO == 0.75 else 32) if P1GO >= 0.75 else 0):
                bank = PSF[m % 4]
                if m >= 4:
                    sm_, val_ = evdone[m - 4]
                    te.wait_ge(sm_, val_)
                bias_part(bank[:], bb_a[0:1, m * 128:(m + 1) * 128])
                warr = wA if m < 16 else wB
                moff = (m if m < 16 else m - 16) * 128
                for k in range(8):
                    te.matmul(bank[:], warr[:, k * 2048 + moff: k * 2048 + moff + 128],
                              xsT[:, k * 512:(k + 1) * 512], start=False, stop=(k == 7))
                tv = tmark()
                if m < 16:
                    ve.wait_ge(sT, tv)
                    vv = inc(ve.tensor_copy(ust(m), bank[:]), sV)
                    evdone[m] = (sV, vv)
                else:
                    se.wait_ge(sT, tv)
                    if m >= 20:
                        se.wait_ge(sE, v(sE))
                    slot = zbuf[:, (m % 4) * 512:(m % 4 + 1) * 512]
                    sv2 = inc(se.activation(slot, bank[:], AF.Silu), sS)
                    evdone[m] = (sS, sv2)
                    if stage != "p1z":
                        syq.wait_ge(sS, sv2)
                        zdma[m] = inc(syq.dma_start(
                            out=dap(zsT_s, (m - 16) * 128 * T + st * STT_, [[T, 128], [1, 512]]),
                            in_=slot), sE, 16)
            nc.multi_engine_barrier(ALLE)
            if P1GO >= 2:
                pass
            inc(ve.tensor_copy(uprev[:].rearrange("p (m c) -> p m c", c=515)[:, :, 0:3],
                               tailb[:].rearrange("p (m c) -> p m c", c=3)), sV)
            inc(ve.tensor_copy(tailb[:].rearrange("p (m c) -> p m c", c=3),
                               uprev[:].rearrange("p (m c) -> p m c", c=515)[:, :, 512:515]), sV)
            nc.multi_engine_barrier(ALLE)
            for m in range(16 if P1GO >= 2 else 0):
                bank = PSF[m % 4]
                if m >= 4:
                    sm_, val_ = evdone[32 + m - 4]
                    te.wait_ge(sm_, val_)
                bias_part(bank[:], cbrow[:, m * 128:(m + 1) * 128])
                for j in range(4):
                    te.matmul(bank[:], wA[:, 16384 + (j * 16 + m) * 128:16384 + (j * 16 + m + 1) * 128],
                              uprev[:, m * 515 + j: m * 515 + j + 512], start=False, stop=(j == 3))
                tv = tmark()
                se.wait_ge(sT, tv)
                sv2 = inc(se.activation(ust(m), bank[:], AF.Silu), sS)
                evdone[32 + m] = (sS, sv2)
                syq.wait_ge(sS, sv2)
                inc(syq.dma_start(out=dap(uT_s, m * 128 * T + st * STT_, [[T, 128], [1, 512]]),
                                  in_=ust(m)), sE, 16)
            nc.multi_engine_barrier(ALLE)
            if P1GO >= 3:
                bias_part(pE[0:64, :], bb_a[0:1, 4096:4160], 512)
            for k in range(16 if P1GO >= 3 else 0):
                te.matmul(pE[0:64, :], wB[:, 18944 + k * 64: 18944 + (k + 1) * 64],
                          ust(k), start=False, stop=(k == 15))
            if P1GO >= 3:
                tv = tmark()
                ve.wait_ge(sT, tv)
                inc(ve.tensor_copy(dtpre[:], pE[0:64, :]), sV)
                te.wait_ge(sV, v(sV))
            for m in range(16 if P1GO >= 3 else 0):
                bank = PSF[m % 2]
                bias_part(bank[:], db2row[:, m * 128:(m + 1) * 128])
                te.matmul(bank[:], wB[0:64, 16896 + m * 128: 16896 + (m + 1) * 128],
                          dtpre[:], start=False, stop=True)
                tv = tmark()
                ve.wait_ge(sT, tv)
                ve.wait_ge(sE, v(sE))
                vv = inc(ve.tensor_scalar_max(spw[:, 0:512], bank[:], 0.0), sV)
                se.wait_ge(sV, vv)
                inc(se.activation(spw[:, 512:1024], bank[:], AF.Abs), sS)
                inc(se.activation(spw[:, 1024:1536], spw[:, 512:1024], AF.Exp, scale=-1.0), sS)
                inc(se.activation(spw[:, 1024:1536], spw[:, 1024:1536], AF.Ln, bias=onec[:]), sS)
                ve.wait_ge(sS, v(sS))
                inc(ve.tensor_add(spw[:, 0:512], spw[:, 0:512], spw[:, 1024:1536]), sV)
                dtbf = zbuf[:, (m % 2) * 512:(m % 2) * 512 + 512]
                vv = inc(ve.tensor_scalar(dtbf, spw[:, 0:512], 0.1, 0.001, ALU.min, ALU.max), sV)
                syq.wait_ge(sV, vv)
                inc(syq.dma_start(out=dap(dtT_s, m * 128 * T + st * STT_, [[T, 128], [1, 512]]),
                                  in_=dtbf), sE, 16)
                te.wait_ge(sV, vv)
            if P1GO >= 4:
                bias_part(pF[0:32, :], bb_a[0:1, 4160:4192], 512)
                for k in range(16):
                    te.matmul(pF[0:32, :], wB[:, 16384 + k * 32: 16384 + (k + 1) * 32],
                              ust(k), start=False, stop=(k == 15))
                tv = tmark()
                se.wait_ge(sT, tv)
                bcbf = zbuf[:, 1024:1536]
                sv2 = inc(se.activation(bcbf[0:32, :], pF[0:32, :], AF.Tanh), sS)
                syq.wait_ge(sS, sv2)
                inc(syq.dma_start(out=dap(bT_s, st * STT_, [[T, 16], [1, 512]]), in_=bcbf[0:16, :]), sE, 16)
                inc(syq.dma_start(out=dap(cT_s, st * STT_, [[T, 16], [1, 512]]), in_=bcbf[16:32, :]), sE, 16)
            gpq.wait_ge(sG, v(sG))
            syq.wait_ge(sD, v(sD))
            syq.wait_ge(sE, v(sE))

        if stage.startswith("p2") or stage in ("mamba", "full"):
          P2GO = {"p2a": 0, "p2n": 0, "p2b": 1, "p2c": 2, "p2x": 9, "p2y": 9}.get(stage, 9)
          DGN = 1 if stage == "p2x" else (4 if stage == "p2y" else 16)
          clr()
          inc(syq.dma_start(out=wB[:, 0:2048], in_=H["selm"]), sD, 16)
          inc(syq.dma_start(out=wB[:, 2048:4096], in_=H["collm"]), sD, 16)
          te.wait_ge(sD, v(sD))
          nc.all_engine_barrier()
          c_sel = wB[:, 0:2048]
          c_coll = wB[:, 2048:4096]
          for dg in range(DGN):
            with nc.Fori(0, T // 512) as ts:
                clr()
                toff = ts * 512
                inc(syq.dma_start(out=sc_dt[:], in_=dap(dtT_s, dg * 128 * T + toff, [[T, 128], [1, 512]])), sD, 16)
                inc(syq.dma_start(out=sc_u[:], in_=dap(uT_s, dg * 128 * T + toff, [[T, 128], [1, 512]])), sD, 16)
                inc(syq.dma_start(out=sc_zs[:], in_=dap(zsT_s, dg * 128 * T + toff, [[T, 128], [1, 512]])), sD, 16)
                inc(syq.dma_start(out=sc_bt[:], in_=dap(bT_s, toff, [[T, 16], [1, 512]])), sD, 16)
                inc(syq.dma_start(out=sc_ct[:], in_=dap(cT_s, toff, [[T, 16], [1, 512]])), sD, 16)
                ve.wait_ge(sD, v(sD))
                inc(ve.tensor_mul(sc_u2[:], sc_dt[:], sc_u[:]), sV)
                te.wait_ge(sD, v(sD))
                te.wait_ge(sV, v(sV))
                scan_done = {}
                coll_done = {}
                for j in range(16 if P2GO >= 1 else 0):
                    g_ = dg * 16 + j
                    hf = j % 2
                    peb = pA if hf == 0 else pB
                    pub = pC if hf == 0 else pD
                    if j >= 2:
                        te.wait_ge(sV, scan_done[j - 2])
                    te.matmul(peb[:], c_sel[:, j * 128:(j + 1) * 128], sc_dt[:], start=True, stop=True)
                    te.matmul(pub[:], c_sel[:, j * 128:(j + 1) * 128], sc_u2[:], start=True, stop=True)
                    tv = tmark()
                    se.wait_ge(sT, tv)
                    if j >= 2:
                        se.wait_ge(sV, scan_done[j - 2])
                    inc(se.activation(sc_da[:, hf * 512:(hf + 1) * 512], peb[:], AF.Exp,
                                      scale=c_acols[:, g_:g_ + 1]), sS)
                    inc(se.activation(sc_u3[:, hf * 512:(hf + 1) * 512], pub[:], AF.Copy), sS)
                    te.wait_ge(sS, v(sS))
                    if P2GO == 1:
                        scan_done[j] = v(sV)
                        coll_done[j] = v(sT)
                        continue
                    if j >= 1:
                        te.wait_ge(sV, scan_done[j - 1])
                    te.matmul(pF[:], c_ssel[:], sc_bt[:], start=True, stop=True)
                    tv = tmark()
                    ve.wait_ge(sT, tv)
                    ve.wait_ge(sS, v(sS))
                    inc(ve.tensor_mul(sc_db[:, hf * 512:(hf + 1) * 512], pF[:],
                                      sc_u3[:, hf * 512:(hf + 1) * 512]), sV)
                    if P2GO != 2:
                        inc(ve.tensor_tensor_scan(sc_h[:, hf * 512:(hf + 1) * 512],
                                                  sc_da[:, hf * 512:(hf + 1) * 512],
                                                  sc_db[:, hf * 512:(hf + 1) * 512],
                                                  hstate[:, g_:g_ + 1], ALU.mult, ALU.add), sV)
                    else:
                        inc(ve.tensor_copy(sc_h[:, hf * 512:(hf + 1) * 512],
                                           sc_db[:, hf * 512:(hf + 1) * 512]), sV)
                    inc(ve.tensor_copy(hstate[:, g_:g_ + 1], sc_h[:, hf * 512 + 511: hf * 512 + 512]), sV)
                    te.wait_ge(sV, v(sV))
                    te.matmul(pF[:], c_ssel[:], sc_ct[:], start=True, stop=True)
                    tv = tmark()
                    ve.wait_ge(sT, tv)
                    if j >= 2:
                        ve.wait_ge(sT, coll_done[j - 2])
                    vv = inc(ve.tensor_mul(sc_q[:, hf * 512:(hf + 1) * 512], pF[:],
                                           sc_h[:, hf * 512:(hf + 1) * 512]), sV)
                    scan_done[j] = vv
                    te.wait_ge(sV, vv)
                    te.matmul(pE[:], c_coll[:, j * 128:(j + 1) * 128], sc_q[:, hf * 512:(hf + 1) * 512],
                              start=(j == 0), stop=(j == 15))
                    coll_done[j] = tmark()
                ve.wait_ge(sT, v(sT))
                if P2GO >= 2:
                    inc(ve.scalar_tensor_tensor(sc_y[:], sc_u[:], c_dcols[:, dg:dg + 1], pE[:], ALU.mult, ALU.add), sV)
                else:
                    inc(ve.tensor_copy(sc_y[:], sc_u[:]), sV)
                vv = inc(ve.tensor_mul(sc_g[:], sc_y[:], sc_zs[:]), sV)
                syq.wait_ge(sV, vv)
                inc(syq.dma_start(out=dap(ygT_s, dg * 128 * T + toff, [[T, 128], [1, 512]]), in_=sc_g[:]), sE, 16)
                syq.wait_ge(sE, v(sE))

        if stage in ("mamba", "full"):
          clr()
          wload(wB[:, 0:16384], H["opw"], 16, 1024)
          te.wait_ge(sV, v(sV))
          nc.all_engine_barrier()
          with nc.Fori(0, NST) as st:
              clr()
              xoff = st * (STT_ * DM)
              for m in range(16):
                  inc(syq.dma_start(out=ygt[:, m * 512:(m + 1) * 512],
                                    in_=dap(ygT_s, m * 128 * T + st * STT_, [[T, 128], [1, 512]])), sD, 16)
              for q in range(4):
                  inc(syq.dma_start(out=xt[:, q * DM:(q + 1) * DM],
                                    in_=dap(x_in, xoff + q * 128 * DM, [[DM, 128], [1, DM]])), sD, 16)
                  inc(syq.dma_start(out=xt2[:, q * DM:(q + 1) * DM],
                                    in_=dap(xn_s, xoff + q * 128 * DM, [[DM, 128], [1, DM]])), sD, 16)
              te.wait_ge(sD, v(sD))
              ve.wait_ge(sD, v(sD))
              se.wait_ge(sD, v(sD))
              ev_q = {}
              for q in range(4):
                  p1, p2 = PSF[(q % 2) * 2], PSF[(q % 2) * 2 + 1]
                  if q >= 2:
                      te.wait_ge(sV, ev_q[q - 2])
                  bias_free(p1[:], oprow[0:1, 0:512])
                  bias_free(p2[:], oprow[0:1, 512:1024])
                  for k in range(16):
                      te.matmul(p1[:], ygt[:, k * 512 + q * 128: k * 512 + (q + 1) * 128],
                                wB[:, k * 1024: k * 1024 + 512], start=False, stop=(k == 15))
                      te.matmul(p2[:], ygt[:, k * 512 + q * 128: k * 512 + (q + 1) * 128],
                                wB[:, k * 1024 + 512: k * 1024 + 1024], start=False, stop=(k == 15))
                  tv = tmark()
                  ve.wait_ge(sT, tv)
                  ve.wait_ge(sE, v(sE))
                  xq = xt[:, q * DM:(q + 1) * DM]
                  xnq = xt2[:, q * DM:(q + 1) * DM]
                  inc(ve.tensor_add(dump[:, 0:512], xnq[:, 0:512], p1[:]), sV)
                  vv0 = inc(ve.tensor_add(dump[:, 512:1024], xnq[:, 512:1024], p2[:]), sV)
                  ev_q[q] = vv0
                  ln_stats(dump[:], 0, ec5[:], st1, st2, A4dmp[:, 0:1024])
                  inc(ve.tensor_scalar(dump[:], dump[:], st1[:, 0:1], st2[:, 0:1], ALU.subtract, ALU.mult), sV)
                  inc(ve.tensor_add(xq, xq, dump[:]), sV)
                  syq.wait_ge(sV, v(sV))
                  inc(syq.dma_start(out=dap(xres, xoff + q * 128 * DM, [[DM, 128], [1, DM]]), in_=xq), sE, 16)
                  ln_stats(xq, 1, ec5[:], st1, st2, A4dmp[:, 0:1024])
                  vv = inc(ve.tensor_scalar(dump[:], xq, st1[:, 1:2], st2[:, 1:2], ALU.subtract, ALU.mult), sV)
                  syq.wait_ge(sV, vv)
                  inc(syq.dma_start(out=dap(xinner, xoff + q * 128 * DM, [[DM, 128], [1, DM]]), in_=dump[:]), sE, 16)
              syq.wait_ge(sD, v(sD))
              syq.wait_ge(sE, v(sE))

        if stage != "full":
            clr()
            with nc.Fori(0, NST) as st:
                clr()
                xoff = st * (STT_ * DM)
                for q in range(4):
                    inc(syq.dma_start(out=xt[:, q * DM:(q + 1) * DM],
                                      in_=dap(xres, xoff + q * 128 * DM, [[DM, 128], [1, DM]])), sD, 16)
                syq.wait_ge(sD, v(sD))
                for q in range(4):
                    inc(syq.dma_start(out=dap(out, xoff + q * 128 * DM, [[DM, 128], [1, DM]]),
                                      in_=xt[:, q * DM:(q + 1) * DM]), sE, 16)
                syq.wait_ge(sD, v(sD))
                syq.wait_ge(sE, v(sE))
            nc.all_engine_barrier()

        if stage == "full":
            def attn_load(l):
                clr()
                wload(wA[:], H[f"wqkv{l}"], 8, 3072)
                wload(wB[:, 0:8192], H[f"wo{l}"], 8, 1024)
                te.wait_ge(sV, v(sV))
                repl128(r_qsA, H[f"qs{l}"], 64)
                repl128(r_ksA, H[f"ks{l}"], 64)
                ve.wait_ge(sV, v(sV))
                inc(ve.tensor_copy(r_qsB[:, 0:32], r_qsA[:, 32:64]), sV)
                inc(ve.tensor_copy(r_qsB[:, 32:64], r_qsA[:, 0:32]), sV)
                inc(ve.tensor_copy(r_ksB[:, 0:32], r_ksA[:, 32:64]), sV)
                inc(ve.tensor_copy(r_ksB[:, 32:64], r_ksA[:, 0:32]), sV)
                inc(ve.memset(kwin[:], 0.0), sV)
                inc(ve.memset(vbuf[:], 0.0), sV)
                clr()

            def attn_body(st_val, first):
                clr()
                xoff = st_val * (STT_ * DM)
                inc(ve.tensor_copy(kwin[:].rearrange("p (c t) -> p c t", t=768)[:, :, 0:256],
                                   kwin[:].rearrange("p (c t) -> p c t", t=768)[:, :, 512:768]), sV)
                inc(ve.tensor_copy(vb(0), vb(4)), sV)
                inc(ve.tensor_copy(vb(1), vb(5)), sV)
                for q in range(4):
                    inc(syq.dma_start(out=xt[:, q * DM:(q + 1) * DM],
                                      in_=dap(xinner, xoff + q * 128 * DM, [[DM, 128], [1, DM]])), sD, 16)
                    if first and q == 0:
                        inc(syq.dma_start(out=xt2[1:128, 0:512], in_=dap(xinner, 512, [[DM, 127], [1, 512]])), sD, 16)
                    else:
                        inc(syq.dma_start(out=xt2[:, q * 512:(q + 1) * 512],
                                          in_=dap(xinner, xoff + q * 128 * DM - DM + 512, [[DM, 128], [1, 512]])), sD, 16)
                ve.wait_ge(sD, v(sD))
                if first:
                    inc(ve.memset(xt2[0:1, 0:512], 0.0), sV)
                for q in range(4):
                    inc(ve.tensor_copy(xb[:, q * DM:(q + 1) * DM], xt[:, q * DM:(q + 1) * DM]), sV)
                    inc(ve.tensor_copy(xpb[:, q * 512:(q + 1) * 512], xt2[:, q * 512:(q + 1) * 512]), sV)
                nc.multi_engine_barrier(ALLE)
                for cc in range(8):
                    for q in range(4):
                        srcx = (xb[:, q * DM + cc * 128: q * DM + (cc + 1) * 128] if cc < 4 else
                                xpb[:, q * 512 + (cc - 4) * 128: q * 512 + (cc - 3) * 128])
                        te.transpose(pT1[:, q * 128:(q + 1) * 128], srcx, c_ident[:])
                    tv = tmark()
                    ve.wait_ge(sT, tv)
                    inc(ve.tensor_copy(xsT[:, cc * 512:(cc + 1) * 512], pT1[:]), sV)
                    te.wait_ge(sV, v(sV))
                nc.multi_engine_barrier(ALLE)
                tabg = [0]
                qkg = [0]
                for q in range(4):
                    for nn in range(6):
                        for k in range(8):
                            te.matmul(PSF[nn][:], xsT[:, k * 512 + q * 128: k * 512 + (q + 1) * 128],
                                      wA[:, k * 3072 + nn * 512: k * 3072 + (nn + 1) * 512],
                                      start=(k == 0), stop=(k == 7))
                    tv = tmark()
                    ve.wait_ge(sT, tv)
                    if q > 0:
                        ve.wait_ge(sS, qkg[0])
                    inc(ve.tensor_copy(qkt_b[:, 0:512], pA[:]), sV)
                    inc(ve.tensor_copy(qkt_b[:, 512:1024], pB[:]), sV)
                    inc(ve.tensor_copy(qkt_b[:, 1024:1536], pC[:]), sV)
                    inc(ve.tensor_copy(qkt_b[:, 1536:2048], pD[:]), sV)
                    inc(ve.tensor_copy(vb(2 + q)[:, 0:512], pE[:]), sV)
                    vv = inc(ve.tensor_copy(vb(2 + q)[:, 512:1024], pF[:]), sV)
                    te.wait_ge(sV, vv)
                    for which in range(2):
                        if q > 0 or which > 0:
                            syq.wait_ge(sV, tabg[0])
                        srcq = qkt_b[:, which * 1024:(which + 1) * 1024]
                        tabA_h = H["tabAq"] if which == 0 else H["tabAk"]
                        tabB_h = H["tabBq"] if which == 0 else H["tabBk"]
                        rA = r_qsA if which == 0 else r_ksA
                        rB = r_qsB if which == 0 else r_ksB
                        inc(syq.dma_start(out=tabt[:, 0:64],
                                          in_=dap(tabA_h, (st_val * 512 + q * 128) * DH, [[DH, 128], [1, DH]])), sD, 16)
                        inc(syq.dma_start(out=tabt[:, 64:128],
                                          in_=dap(tabB_h, (st_val * 512 + q * 128) * DH, [[DH, 128], [1, DH]])), sD, 16)
                        se.wait_ge(sV, v(sV))
                        sq2 = inc(se.activation(dump[:], srcq, AF.Square), sS)
                        qkg[0] = sq2
                        ve.wait_ge(sS, sq2)
                        inc(ve.reduce_sum(st1[:, 0:16], dump[:].rearrange("p (h d) -> p h d", d=DH), axis=AX.X), sV)
                        se.wait_ge(sV, v(sV))
                        inc(se.activation(st1[:, 0:16], st1[:, 0:16], AF.Ln, bias=ec24[:]), sS)
                        inc(se.activation(st1[:, 0:16], st1[:, 0:16], AF.Exp, scale=mhc[:]), sS)
                        ve.wait_ge(sS, v(sS))
                        ve.wait_ge(sD, v(sD))
                        inc(ve.tensor_tensor(qn_b[:].rearrange("p (h d) -> p h d", d=DH),
                                             srcq.rearrange("p (h d) -> p h d", d=DH),
                                             st1[:, 0:16].unsqueeze(2).broadcast_to([128, 16, DH]), ALU.mult), sV)
                        inc(ve.tensor_mul(rot2[:, 0:64], tabt[:, 0:64], rA[:]), sV)
                        inc(ve.tensor_mul(rot2[:, 64:128], tabt[:, 64:128], rB[:]), sV)
                        inc(ve.tensor_tensor(qd_b[:].rearrange("p (h d) -> p h d", d=DH),
                                             qn_b[:].rearrange("p (h d) -> p h d", d=DH),
                                             rot2[:, 0:64].unsqueeze(1).broadcast_to([128, 16, DH]), ALU.mult), sV)
                        inc(ve.tensor_tensor(qr_b[:].rearrange("p (h d) -> p h d", d=DH)[:, :, 0:32],
                                             qn_b[:].rearrange("p (h d) -> p h d", d=DH)[:, :, 32:64],
                                             rot2[:, 64:96].unsqueeze(1).broadcast_to([128, 16, 32]), ALU.mult), sV)
                        inc(ve.tensor_tensor(qr_b[:].rearrange("p (h d) -> p h d", d=DH)[:, :, 32:64],
                                             qn_b[:].rearrange("p (h d) -> p h d", d=DH)[:, :, 0:32],
                                             rot2[:, 96:128].unsqueeze(1).broadcast_to([128, 16, 32]), ALU.mult), sV)
                        tabg[0] = inc(ve.tensor_add(qp_t[:, which * 1024:(which + 1) * 1024], qd_b[:], qr_b[:]), sV)
                    nc.multi_engine_barrier(ALLE)
                    for piece in range(2):
                        for cc in range(4):
                            te.transpose(pT1[:, cc * 128:(cc + 1) * 128],
                                         qp_t[:, piece * 512 + cc * 128: piece * 512 + (cc + 1) * 128], c_ident[:])
                            te.transpose(pT2[:, cc * 128:(cc + 1) * 128],
                                         qp_t[:, 1024 + piece * 512 + cc * 128: 1024 + piece * 512 + (cc + 1) * 128],
                                         c_ident[:])
                        tv = tmark()
                        ve.wait_ge(sT, tv)
                        for cc in range(4):
                            c2 = piece * 4 + cc
                            inc(ve.tensor_copy(qT[:, c2 * 512 + q * 128: c2 * 512 + (q + 1) * 128],
                                               pT1[:, cc * 128:(cc + 1) * 128]), sV)
                            inc(ve.tensor_copy(kwin[:, c2 * 768 + 256 + q * 128: c2 * 768 + 256 + (q + 1) * 128],
                                               pT2[:, cc * 128:(cc + 1) * 128]), sV)
                        te.wait_ge(sV, v(sV))
                nc.multi_engine_barrier(ALLE)
                pn_done = {}
                att_done = {}
                avv_done = {}
                tp_done = {}
                rp_done = {}
                wo_done = {}
                for u in range(64):
                    qt, h = u // 16, u % 16
                    cc = h // 2
                    pb_ = 64 * (h % 2)
                    scb = PSF[u % 2]
                    avb = PSF[2 + u % 2]
                    if u >= 2:
                        te.wait_ge(sS, pn_done[u - 2])
                    if u >= 4:
                        ve.wait_ge(sS, pn_done[u - 4])
                    mi = 2
                    if first and qt == 0:
                        mi = 0
                    if first and qt == 1:
                        mi = 1
                    te.matmul(scb[:, 0:384], qT[pb_:pb_ + 64, cc * 512 + qt * 128: cc * 512 + (qt + 1) * 128],
                              kwin[pb_:pb_ + 64, cc * 768 + qt * 128: cc * 768 + qt * 128 + 384],
                              start=True, stop=False)
                    te.matmul(scb[:, 0:384], c_ident[:], c_masks[:, mi * 384:(mi + 1) * 384], start=False, stop=True)
                    tv = tmark()
                    ve.wait_ge(sT, tv)
                    inc(ve.reduce_max(smax[:, u % 4:u % 4 + 1], scb[:, 0:384], axis=AX.X, negate=True), sV)
                    se.wait_ge(sV, v(sV))
                    if u >= 2:
                        se.wait_ge(sT, tp_done[u - 2])
                    if u >= 4:
                        se.wait_ge(sV, rp_done[u - 4])
                    pv = inc(se.activation(pn[:, (u % 2) * 384:(u % 2 + 1) * 384], scb[:, 0:384], AF.Exp,
                                           bias=smax[:, u % 4:u % 4 + 1], accum_out=szv[:, u % 4:u % 4 + 1]), sS)
                    pn_done[u] = pv
                    ve.wait_ge(sS, pv)
                    rp_done[u] = inc(ve.reciprocal(srp[:, u % 4:u % 4 + 1], szv[:, u % 4:u % 4 + 1]), sV)
                    vv = inc(ve.tensor_scalar_mul(pn[:, (u % 2) * 384:(u % 2 + 1) * 384],
                                                  pn[:, (u % 2) * 384:(u % 2 + 1) * 384], srp[:, u % 4:u % 4 + 1]), sV)
                    te.wait_ge(sV, vv)
                    ptp = pT1 if u % 2 == 0 else pT2
                    if u >= 2:
                        te.wait_ge(sV, att_done[u - 2])
                    for c3 in range(3):
                        te.transpose(ptp[:, c3 * 128:(c3 + 1) * 128],
                                     pn[:, (u % 2) * 384 + c3 * 128:(u % 2) * 384 + (c3 + 1) * 128], c_ident[:])
                    tv = tmark()
                    tp_done[u] = tv
                    ve.wait_ge(sT, tv)
                    if u >= 4:
                        ve.wait_ge(sT, avv_done[u - 4])
                    av_ = inc(ve.tensor_copy(attnT[:, (u % 4) * 384:(u % 4 + 1) * 384], ptp[:, 0:384]), sV)
                    att_done[u] = av_
                    te.wait_ge(sV, av_)
                    if u >= 2:
                        te.wait_ge(sV, wo_done[u - 2])
                    for c3 in range(3):
                        te.matmul(avb[0:64, 0:128], vb(qt + c3)[:, h * 64:(h + 1) * 64],
                                  attnT[:, (u % 4) * 384 + c3 * 128:(u % 4) * 384 + (c3 + 1) * 128],
                                  start=(c3 == 0), stop=(c3 == 2))
                    avv_done[u] = tmark()
                    ve.wait_ge(sT, avv_done[u])
                    wo_done[u] = inc(ve.tensor_copy(
                        woT[pb_:pb_ + 64, cc * 512 + qt * 128: cc * 512 + (qt + 1) * 128],
                        avb[0:64, 0:128]), sV)
                nc.multi_engine_barrier(ALLE)
                ev_wo = {}
                for q in range(4):
                    p1, p2 = PSF[(q % 2) * 2], PSF[(q % 2) * 2 + 1]
                    if q >= 2:
                        te.wait_ge(sV, ev_wo[q - 2])
                    for k in range(8):
                        te.matmul(p1[:], woT[:, k * 512 + q * 128: k * 512 + (q + 1) * 128],
                                  wB[:, k * 1024: k * 1024 + 512], start=(k == 0), stop=(k == 7))
                        te.matmul(p2[:], woT[:, k * 512 + q * 128: k * 512 + (q + 1) * 128],
                                  wB[:, k * 1024 + 512: k * 1024 + 1024], start=(k == 0), stop=(k == 7))
                    tv = tmark()
                    ve.wait_ge(sT, tv)
                    inc(ve.tensor_add(xt[:, q * DM: q * DM + 512], xt[:, q * DM: q * DM + 512], p1[:]), sV)
                    vv = inc(ve.tensor_add(xt[:, q * DM + 512:(q + 1) * DM], xt[:, q * DM + 512:(q + 1) * DM], p2[:]), sV)
                    ev_wo[q] = vv
                    syq.wait_ge(sV, vv)
                    inc(syq.dma_start(out=dap(xinner, xoff + q * 128 * DM, [[DM, 128], [1, DM]]),
                                      in_=xt[:, q * DM:(q + 1) * DM]), sE, 16)
                syq.wait_ge(sD, v(sD))
                syq.wait_ge(sE, v(sE))
                gpq.wait_ge(sG, v(sG))

            def ffn1_load(l):
                clr()
                src3 = H[f"w1p{l}"].rearrange("(c p) n -> p c n", p=128)
                for c in range(8):
                    dv = inc(syq.dma_start(out=wF[:, 0:2816], in_=src3[:, c, 0:FF]), sD, 16)
                    ve.wait_ge(sD, dv)
                    vv = inc(ve.tensor_copy(wB[:, c * FF:(c + 1) * FF], wF[:, 0:2816]), sV)
                    syq.wait_ge(sV, vv)
                    dv = inc(syq.dma_start(out=wF[:, 0:2816], in_=src3[:, c, FF:2 * FF]), sD, 16)
                    ve.wait_ge(sD, dv)
                    vv = inc(ve.tensor_copy(wA[:, c * FF:(c + 1) * FF], wF[:, 0:2816]), sV)
                    syq.wait_ge(sV, vv)
                rowload(b1row, H[f"b1p{l}"], 5632)
                rowload(b2row, H[f"b2{l}"], 1024)
                te.wait_ge(sV, v(sV))
                repl128(r_lng, H[f"lng{l}"], 1024)
                repl128(r_lnb, H[f"lnb{l}"], 1024)
                clr()

            def ffn1_body(st_val, first):
                clr()
                xoff = st_val * (STT_ * DM)
                for q in range(4):
                    inc(syq.dma_start(out=xt2[:, q * DM: q * DM + 512],
                                      in_=dap(xinner, xoff + q * 128 * DM, [[DM, 128], [1, 512]])), sD, 16)
                    if first and q == 0:
                        inc(syq.dma_start(out=xt2[1:128, 512:1024],
                                          in_=dap(xinner, 512, [[DM, 127], [1, 512]])), sD, 16)
                    else:
                        inc(syq.dma_start(out=xt2[:, q * DM + 512:(q + 1) * DM],
                                          in_=dap(xinner, xoff + q * 128 * DM - DM + 512, [[DM, 128], [1, 512]])), sD, 16)
                ve.wait_ge(sD, v(sD))
                se.wait_ge(sD, v(sD))
                if first:
                    inc(ve.memset(xt2[0:1, 512:1024], 0.0), sV)
                for q in range(4):
                    xsq = xt2[:, q * DM:(q + 1) * DM]
                    ln_stats(xsq, q, ec5[:], st1, st2, A4dmp[:, 0:1024])
                    inc(ve.tensor_scalar(xt[:, q * DM:(q + 1) * DM], xsq,
                                         st1[:, q:q + 1], st2[:, q:q + 1], ALU.subtract, ALU.mult), sV)
                    inc(ve.tensor_mul(xt[:, q * DM:(q + 1) * DM], xt[:, q * DM:(q + 1) * DM], r_lng[:]), sV)
                    inc(ve.tensor_add(xb[:, q * DM:(q + 1) * DM], xt[:, q * DM:(q + 1) * DM], r_lnb[:]), sV)
                nc.multi_engine_barrier(ALLE)
                for cc in range(8):
                    for q in range(4):
                        te.transpose(pT1[:, q * 128:(q + 1) * 128],
                                     xb[:, q * DM + cc * 128: q * DM + (cc + 1) * 128], c_ident[:])
                    tv = tmark()
                    ve.wait_ge(sT, tv)
                    inc(ve.tensor_copy(xsT[:, cc * 512:(cc + 1) * 512], pT1[:]), sV)
                    te.wait_ge(sV, v(sV))
                nc.multi_engine_barrier(ALLE)
                msd = {}
                for i in range(22):
                    pa_, pg_ = PSF[(i % 2) * 2], PSF[(i % 2) * 2 + 1]
                    if i >= 2:
                        te.wait_ge(sV, msd[i - 2][0])
                    bias_part(pa_[:], b1row[0:1, i * 128:(i + 1) * 128])
                    bias_part(pg_[:], b1row[0:1, 2816 + i * 128: 2816 + (i + 1) * 128])
                    for k in range(8):
                        te.matmul(pa_[:], wB[:, k * 2816 + i * 128: k * 2816 + (i + 1) * 128],
                                  xsT[:, k * 512:(k + 1) * 512], start=False, stop=(k == 7))
                        te.matmul(pg_[:], wA[:, k * 2816 + i * 128: k * 2816 + (i + 1) * 128],
                                  xsT[:, k * 512:(k + 1) * 512], start=False, stop=(k == 7))
                    tv = tmark()
                    se.wait_ge(sT, tv)
                    if i >= 2:
                        se.wait_ge(sV, msd[i - 2][0])
                    sv2 = inc(se.activation(gsb2[:, (i % 2) * 512:(i % 2 + 1) * 512], pg_[:], AF.Gelu), sS)
                    ve.wait_ge(sT, tv)
                    inc(ve.tensor_copy(asb2[:, (i % 2) * 512:(i % 2 + 1) * 512], pa_[:]), sV)
                    ve.wait_ge(sS, sv2)
                    if i >= 2:
                        ve.wait_ge(sE, v(sE))
                    vv = inc(ve.tensor_mul(mso2[:, (i % 2) * 512:(i % 2 + 1) * 512],
                                           asb2[:, (i % 2) * 512:(i % 2 + 1) * 512],
                                           gsb2[:, (i % 2) * 512:(i % 2 + 1) * 512]), sV)
                    syq.wait_ge(sV, vv)
                    gv = inc(syq.dma_start(out=dap(msbT_s, i * 128 * T + st_val * STT_, [[T, 128], [1, 512]]),
                                           in_=mso2[:, (i % 2) * 512:(i % 2 + 1) * 512]), sE, 16)
                    msd[i] = (vv, gv)
                syq.wait_ge(sE, v(sE))
                syq.wait_ge(sD, v(sD))

            def ffn2_load(l):
                clr()
                wload(wB[:], H[f"w2p{l}"], 22, 1024)
                te.wait_ge(sV, v(sV))
                nc.all_engine_barrier()

            def ffn2_body(st_val):
                clr()
                xoff = st_val * (STT_ * DM)
                for m in range(22):
                    inc(syq.dma_start(out=wA[:, m * 512:(m + 1) * 512],
                                      in_=dap(msbT_s, m * 128 * T + st_val * STT_, [[T, 128], [1, 512]])), sD, 16)
                for q in range(4):
                    inc(syq.dma_start(out=xt[:, q * DM:(q + 1) * DM],
                                      in_=dap(xinner, xoff + q * 128 * DM, [[DM, 128], [1, DM]])), sD, 16)
                te.wait_ge(sD, v(sD))
                ve.wait_ge(sD, v(sD))
                ev2 = {}
                for q in range(4):
                    p1, p2 = PSF[(q % 2) * 2], PSF[(q % 2) * 2 + 1]
                    if q >= 2:
                        te.wait_ge(sV, ev2[q - 2])
                    bias_free(p1[:], b2row[0:1, 0:512])
                    bias_free(p2[:], b2row[0:1, 512:1024])
                    for k in range(22):
                        te.matmul(p1[:], wA[:, k * 512 + q * 128: k * 512 + (q + 1) * 128],
                                  wB[:, k * 1024: k * 1024 + 512], start=False, stop=(k == 21))
                        te.matmul(p2[:], wA[:, k * 512 + q * 128: k * 512 + (q + 1) * 128],
                                  wB[:, k * 1024 + 512: k * 1024 + 1024], start=False, stop=(k == 21))
                    tv = tmark()
                    ve.wait_ge(sT, tv)
                    inc(ve.tensor_add(xt[:, q * DM: q * DM + 512], xt[:, q * DM: q * DM + 512], p1[:]), sV)
                    vv = inc(ve.tensor_add(xt[:, q * DM + 512:(q + 1) * DM], xt[:, q * DM + 512:(q + 1) * DM], p2[:]), sV)
                    ev2[q] = vv
                    syq.wait_ge(sV, vv)
                    inc(syq.dma_start(out=dap(xinner, xoff + q * 128 * DM, [[DM, 128], [1, DM]]),
                                      in_=xt[:, q * DM:(q + 1) * DM]), sE, 16)
                syq.wait_ge(sD, v(sD))
                syq.wait_ge(sE, v(sE))

            for l in range(6):
                attn_load(l)
                attn_body(0, True)
                if NST > 1:
                    with nc.Fori(1, NST) as st:
                        attn_body(st, False)
                ffn1_load(l)
                ffn1_body(0, True)
                if NST > 1:
                    with nc.Fori(1, NST) as st:
                        ffn1_body(st, False)
                ffn2_load(l)
                with nc.Fori(0, NST) as st:
                    ffn2_body(st)
                clr()

            with nc.Fori(0, NST) as st:
                clr()
                xoff = st * (STT_ * DM)
                for q in range(4):
                    inc(syq.dma_start(out=xt[:, q * DM:(q + 1) * DM],
                                      in_=dap(xres, xoff + q * 128 * DM, [[DM, 128], [1, DM]])), sD, 16)
                    inc(syq.dma_start(out=xt2[:, q * DM:(q + 1) * DM],
                                      in_=dap(xinner, xoff + q * 128 * DM, [[DM, 128], [1, DM]])), sD, 16)
                ve.wait_ge(sD, v(sD))
                se.wait_ge(sD, v(sD))
                for q in range(4):
                    xiq = xt2[:, q * DM:(q + 1) * DM]
                    ln_stats(xiq, q, ec5[:], st1, st2, A4dmp[:, 0:1024])
                    inc(ve.tensor_scalar(dump[:], xiq, st1[:, q:q + 1], st2[:, q:q + 1], ALU.subtract, ALU.mult), sV)
                    vv = inc(ve.tensor_add(xt[:, q * DM:(q + 1) * DM], xt[:, q * DM:(q + 1) * DM], dump[:]), sV)
                    syq.wait_ge(sV, vv)
                    inc(syq.dma_start(out=dap(out, xoff + q * 128 * DM, [[DM, 128], [1, DM]]),
                                      in_=xt[:, q * DM:(q + 1) * DM]), sE, 16)
                syq.wait_ge(sD, v(sD))
                syq.wait_ge(sE, v(sE))
        nc.all_engine_barrier()
    nc.compile()
    return nc


def kernel(**inputs):
    consts = host_consts()
    g = host_prep(inputs)
    nc = build(STAGE)
    x = np.asarray(inputs["L0_attn_x"], dtype=np.float32)
    in_maps = []
    for core in range(8):
        b = core % 4
        m = {k: np.ascontiguousarray(v_) for k, v_ in consts.items()}
        for k, v_ in g.items():
            m[k] = np.ascontiguousarray(v_)
        m["x_in"] = np.ascontiguousarray(x[b])
        in_maps.append(m)
    res = run_bass_kernel_spmd(nc, in_maps, core_ids=list(range(8)))
    outs = [res.results[c]["out"] for c in range(4)]
    return np.stack(outs, axis=0)
